# version 01 (final)
# baseline (speedup 1.0000x reference)
import numpy as np
import concourse.bass as bass
import concourse.mybir as mybir
from concourse.bass_utils import run_bass_kernel_spmd
from contextlib import ExitStack

F32 = mybir.dt.float32
BF16 = mybir.dt.bfloat16
AF = mybir.ActivationFunctionType
ALU = mybir.AluOpType
AX = mybir.AxisListType.X

D = 1024
BATCH = 4
SEQ = 4096
NCORE = 8
R = 2048
NT = R // 128
FC = D // 128
DFF = 4096
JC = DFF // 128
E = 8
NEG_BIG = -1.0e30
LN_EPS = 1e-5
LOG_SCALE = float(np.log(0.125))
TINY = 1e-30
CAPF = 2048.0

ENGINES = ("sync", "tensor", "vector", "scalar", "gpsimd")
DBG_STASH = False
DBG_W2ONE = False
DBG_W2NAIVE = False
DBG_ONEEXP = False
ALL_BLOCKS = ("l0_attn", "l0_ffn", "l1_attn", "l1_ffn")
PAIRS = [[0, 1], [2, 3], [4, 5], [6, 7]]

QT_OFF = 0
KT_OFF = 16384
VG_OFF = KT_OFF + 8192
WS_OFF = KT_OFF
G_OFF = 0
W2_OFF = 32768
ARENA_COLS = 36864


class Sem:
    __slots__ = ("h", "n", "name")

    def __init__(self, h, name):
        self.h = h
        self.n = 0
        self.name = name


class Prog:

    AUTO = ("vector", "scalar")

    def __init__(self, nc):
        self.nc = nc
        self.ops = {e: [] for e in ENGINES}
        self.auto = {}

    def on(self, eng, fn, waits=(), inc=None, inc_n=1):
        if eng in self.AUTO and eng in self.auto:
            sem = self.auto[eng]
            waits = tuple(waits) + ((sem, sem.n),)
            if inc is None:
                inc, inc_n = sem, 1
            else:
                assert inc is sem, f"{eng} op incrementing foreign sem {inc.name}"
        self.ops[eng].append((fn, tuple((s, v) for s, v in waits if v > 0), inc, inc_n))
        if inc is not None:
            inc.n += inc_n

    def mark(self, eng, sem, n=1):
        fn, waits, inc, inc_n = self.ops[eng][-1]
        assert inc is None, f"mark({eng}): last op already increments {inc.name}"
        self.ops[eng][-1] = (fn, waits, sem, n)
        sem.n += n

    def run(self, block):
        def make(eng_name):
            oplist = self.ops[eng_name]

            def body(eng):
                last = {}
                for fn, waits, inc, inc_n in oplist:
                    for sm, v in waits:
                        if last.get(id(sm), 0) < v:
                            eng.wait_ge(sm.h, v)
                            last[id(sm)] = v
                    r = fn(eng)
                    if inc is not None:
                        if r is None:
                            eng.sem_inc(inc.h, inc_n)
                        else:
                            r.then_inc(inc.h, inc_n)
            return body

        for name in ENGINES:
            getattr(block, name)(make(name))


class Pack:
    def __init__(self, nrows):
        self.nrows = nrows
        self.cols = 0
        self.slices = {}
        self.parts = []

    def add(self, name, arr):
        arr = np.asarray(arr, dtype=np.float32)
        assert arr.ndim == 2 and arr.shape[0] == self.nrows, (name, arr.shape)
        self.slices[name] = (self.cols, arr.shape[1])
        self.parts.append(arr)
        self.cols += arr.shape[1]

    def finalize(self):
        return np.ascontiguousarray(np.concatenate(self.parts, axis=1))

    def sl(self, sb, name, c0=0, ncols=None, p0=0, p1=128):
        off, k = self.slices[name]
        if ncols is None:
            ncols = k - c0
        assert c0 + ncols <= k, (name, c0, ncols, k)
        return sb[p0:p1, off + c0: off + c0 + ncols]


def fm_pack(vec, d=D):
    return np.ascontiguousarray(np.asarray(vec, dtype=np.float32).reshape(d // 128, 128).T)


def w_slab(w):
    din, dout = w.shape
    return np.ascontiguousarray(np.asarray(w, dtype=np.float32).reshape(din // 128, 128, dout))


def prep_host(inp):
    x = np.asarray(inp["L0_attn_x"], dtype=np.float32).reshape(BATCH * SEQ, D)

    big = {}
    for nm in ("Wq", "Wk", "Wv", "Wo"):
        big[f"L0a_{nm}"] = w_slab(np.asarray(inp[f"L0_attn_{nm}"]).T)
        big[f"L1a_{nm}"] = w_slab(np.asarray(inp[f"L1_attn_{nm}"]).T)
    for l in range(2):
        for nm in ("wq", "wk", "wv", "wo"):
            big[f"I{l}_{nm}"] = w_slab(np.asarray(inp["L0_ffn_" + nm])[l])
    ew1 = np.asarray(inp["L0_ffn_e_w1"], dtype=np.float32)
    big["I_ew1"] = np.ascontiguousarray(
        ew1.reshape(2, E, FC, 128, JC, 128).transpose(0, 1, 4, 3, 2, 5)
        .reshape(2, E, JC, 128, D))
    ew2 = np.asarray(inp["L0_ffn_e_w2"], dtype=np.float32)
    big["I_ew2"] = np.ascontiguousarray(
        ew2.reshape(2, E, JC, 128, FC, 128).transpose(0, 1, 4, 3, 2, 5)
        .reshape(2, E, FC, 128, DFF))
    big["L1_W1"] = np.stack([w_slab(np.asarray(inp["L1_ffn_W1"])[e]) for e in range(E)])
    big["L1_W2"] = np.stack([w_slab(np.asarray(inp["L1_ffn_W2"])[e]) for e in range(E)])

    pk = Pack(128)
    pk.add("L0a_bq", fm_pack(inp["L0_attn_bq"]))
    pk.add("L0a_bk", fm_pack(inp["L0_attn_bk"]))
    pk.add("L0a_bv", fm_pack(inp["L0_attn_bv"]))
    for l in range(2):
        pk.add(f"I{l}_bq", fm_pack(np.asarray(inp["L0_ffn_bq"])[l]))
        pk.add(f"I{l}_bk", fm_pack(np.asarray(inp["L0_ffn_bk"])[l]))
        pk.add(f"I{l}_bv", fm_pack(np.asarray(inp["L0_ffn_bv"])[l]))
        b1 = np.asarray(inp["L0_ffn_e_b1"])[l]
        pk.add(f"I{l}_b1", np.concatenate([fm_pack(b1[e], DFF) for e in range(E)], axis=1))
    qs_ = np.asarray(inp["L1_attn_q_scale"]).reshape(16, 64)
    ks_ = np.asarray(inp["L1_attn_k_scale"]).reshape(16, 64)
    qsf = np.zeros((128, FC), np.float32)
    ksf = np.zeros((128, FC), np.float32)
    for g in range(FC):
        qsf[0:64, g] = qs_[2 * g]
        qsf[64:128, g] = qs_[2 * g + 1]
        ksf[0:64, g] = ks_[2 * g]
        ksf[64:128, g] = ks_[2 * g + 1]
    pk.add("L1_qs", qsf)
    pk.add("L1_ks", ksf)
    b1l1 = np.asarray(inp["L1_ffn_b1"])
    pk.add("L1_b1", np.concatenate([fm_pack(b1l1[e]) for e in range(E)], axis=1))
    bcp = np.zeros((128, 128), np.float32)
    bcp[0, 0:64] = 1.0
    bcp[1, 64:128] = 1.0
    pk.add("bcpairT_f", bcp)
    cb = np.zeros((128, 3), np.float32)
    cb[:, 0] = LN_EPS
    cb[:, 1] = TINY
    cb[:, 2] = LOG_SCALE
    pk.add("consts", cb)
    smalls = pk.finalize()

    p = np.arange(128)[:, None]
    j = np.arange(512)[None, :]
    stra = [(p + o <= j).astype(np.float32) for o in (0, 128, 256, 384)]
    ones512 = np.ones((128, 512), np.float32)
    base_items = []
    base_items.append(("ident", np.eye(128, dtype=np.float32)))
    base_items.append(("tri", np.tril(np.ones((128, 128), np.float32)).T))
    base_items.append(("ones128t", np.ones((128, 128), np.float32)))
    op_ = np.zeros((128, 2), np.float32)
    op_[0:64, 0] = 1.0
    op_[64:128, 1] = 1.0
    base_items.append(("onespair", op_))
    for l in range(2):
        gw = np.asarray(inp["L0_ffn_gate_w"])[l]
        base_items.append((f"I{l}_gw",
                           gw.reshape(FC, 128, E).transpose(1, 0, 2).reshape(128, FC * E)))
    wr = np.asarray(inp["L1_ffn_Wr"])
    base_items.append(("L1_wr", wr.reshape(FC, 128, E).transpose(1, 0, 2).reshape(128, FC * E)))

    smalls_b_cores = []
    pkb = None
    for c in range(NCORE):
        par = c % 2
        pkbx = Pack(128)
        for nm, arr in base_items:
            pkbx.add(nm, arr)
        pkbx.add("mb0", np.concatenate(stra if par == 0 else [ones512] * 4, axis=1))
        pkbx.add("mb1", np.concatenate([ones512] * 4 if par == 0 else stra, axis=1))
        smalls_b_cores.append(pkbx.finalize())
        pkb = pkbx

    rows_cores = []
    pr = None
    for c in range(NCORE):
        par = c % 2
        prx = Pack(1)
        prx.add("L0a_bo", np.asarray(inp["L0_attn_bo"]).reshape(1, D))
        prx.add("I0_bo", np.asarray(inp["L0_ffn_bo"])[0].reshape(1, D))
        prx.add("I1_bo", np.asarray(inp["L0_ffn_bo"])[1].reshape(1, D))
        prx.add("L1a_bo", np.asarray(inp["L1_attn_bo"]).reshape(1, D))
        for l in range(2):
            prx.add(f"I{l}_ln1w", np.asarray(inp["L0_ffn_ln1_w"])[l].reshape(1, D))
            prx.add(f"I{l}_ln1b", np.asarray(inp["L0_ffn_ln1_b"])[l].reshape(1, D))
            prx.add(f"I{l}_ln2w", np.asarray(inp["L0_ffn_ln2_w"])[l].reshape(1, D))
            prx.add(f"I{l}_ln2b", np.asarray(inp["L0_ffn_ln2_b"])[l].reshape(1, D))
        prx.add("gb0", np.asarray(inp["L0_ffn_gate_b"])[0].reshape(1, E))
        prx.add("gb1", np.asarray(inp["L0_ffn_gate_b"])[1].reshape(1, E))
        prx.add("br", np.asarray(inp["L1_ffn_br"]).reshape(1, E))
        coff = np.zeros((1, 2), np.float32)
        if par == 0:
            coff[0, 0] = NEG_BIG
            coff[0, 1] = NEG_BIG
        prx.add("coff", coff)
        rows_cores.append(prx.finalize())
        pr = prx

    b2s = np.concatenate([np.asarray(inp["L0_ffn_e_b2"])[0],
                          np.asarray(inp["L0_ffn_e_b2"])[1],
                          np.asarray(inp["L1_ffn_b2"])], axis=1).astype(np.float32)

    maps = []
    for c in range(NCORE):
        m = {
            "x_in": np.ascontiguousarray(x[c * R:(c + 1) * R]),
            "smalls": smalls,
            "smalls_b": smalls_b_cores[c],
            "rows_b": rows_cores[c],
            "b2s": b2s,
            "sel_c": np.ascontiguousarray((np.arange(8) < c).astype(np.float32).reshape(8, 1)),
        }
        m.update(big)
        maps.append(m)
    return maps, pk, pkb, pr


class C:
    pass


def build_nc(pk, pkb, pr, blocks=ALL_BLOCKS, taps=()):
    nc = bass.Bass("TRN2", target_bir_lowering=False,
                   detect_race_conditions=False)
    c = C()
    c.nc = nc
    c.pk, c.pkb, c.pr = pk, pkb, pr
    dt = nc.dram_tensor
    c.x_in = dt("x_in", [R, D], F32, kind="ExternalInput").ap()
    c.x_out = dt("x_out", [R, D], F32, kind="ExternalOutput").ap()
    c.smalls_d = dt("smalls", [128, pk.cols], F32, kind="ExternalInput").ap()
    c.smalls_b_d = dt("smalls_b", [128, pkb.cols], F32, kind="ExternalInput").ap()
    c.rows_d = dt("rows_b", [1, pr.cols], F32, kind="ExternalInput").ap()
    c.b2s_d = dt("b2s", [8, 3 * D], F32, kind="ExternalInput").ap()
    c.sel_d = dt("sel_c", [8, 1], F32, kind="ExternalInput").ap()

    c.wd = {}
    shapes = {"I_ew1": [2, E, JC, 128, D], "I_ew2": [2, E, FC, 128, DFF],
              "L1_W1": [E, FC, 128, D], "L1_W2": [E, FC, 128, D]}
    for nm in ("Wq", "Wk", "Wv", "Wo"):
        shapes[f"L0a_{nm}"] = [FC, 128, D]
        shapes[f"L1a_{nm}"] = [FC, 128, D]
    for l in range(2):
        for nm in ("wq", "wk", "wv", "wo"):
            shapes[f"I{l}_{nm}"] = [FC, 128, D]
    for k, shp in shapes.items():
        c.wd[k] = dt(k, shp, F32, kind="ExternalInput").ap()

    c.xres = dt("xres", [R, D], F32, kind="Internal").ap()
    c.k_cc_in = dt("k_cc_in", [D, R], BF16, kind="Internal").ap()
    c.v_cc_in = dt("v_cc_in", [D, R], BF16, kind="Internal").ap()
    c.k_cc_out = dt("k_cc_out", [2 * D, R], BF16, kind="Internal").ap()
    c.v_cc_out = dt("v_cc_out", [2 * D, R], BF16, kind="Internal").ap()
    c.nrm_cc_in = dt("nrm_cc_in", [2, 8], F32, kind="Internal").ap()
    c.nrm_cc_out = dt("nrm_cc_out", [2, 8], F32, kind="Internal").ap()
    c.mean_cc_in = dt("mean_cc_in", [128, FC], F32, kind="Internal").ap()
    c.mean_cc_out = dt("mean_cc_out", [128, FC], F32, kind="Internal").ap()
    c.cnt_cc_in = dt("cnt_cc_in", [8, 1], F32, kind="Internal").ap()
    c.cnt_cc_out = dt("cnt_cc_out", [NCORE * 8, 1], F32, kind="Internal").ap()

    with ExitStack() as ctx:
        en = ctx.enter_context
        c.x_sb = en(nc.sbuf_tensor("x_sb", [128, NT * D], F32))
        c.xnT = en(nc.sbuf_tensor("xnT", [128, FC * R], BF16))
        c.arena = en(nc.sbuf_tensor("arena", [128, ARENA_COLS], BF16))
        c.wring = en(nc.sbuf_tensor("wring", [128, 2 * D], BF16))
        c.sm = en(nc.sbuf_tensor("sm", [128, pk.cols], F32))
        c.smb = en(nc.sbuf_tensor("smb", [128, pkb.cols], BF16))
        c.b2sb = en(nc.sbuf_tensor("b2sb", [8, D], BF16))
        c.selb = en(nc.sbuf_tensor("selb", [8, 1], BF16))
        c.ln_scr = en(nc.sbuf_tensor("ln_scr", [128, D], BF16))
        c.stage = en(nc.sbuf_tensor("stage", [128, 2 * D], BF16))
        c.f32a = en(nc.sbuf_tensor("f32a", [128, 2080], F32))
        c.stat = en(nc.sbuf_tensor("stat", [128, 96], F32))
        c.row_st = en(nc.sbuf_tensor("row_st", [1, D], BF16))
        c.row_f = en(nc.sbuf_tensor("row_f", [2, 64], F32))
        c.row_fb = en(nc.sbuf_tensor("row_fb", [2, 64], BF16))
        c.cnt8 = en(nc.sbuf_tensor("cnt8", [8, 8], BF16))
        c.pw0 = en(nc.psum_tensor("pw0", [128, 1024], F32))
        c.pw1 = en(nc.psum_tensor("pw1", [128, 1024], F32))
        c.pa = en(nc.psum_tensor("pa", [128, 512], F32))
        c.pb = en(nc.psum_tensor("pb", [128, 512], F32))
        c.pt0 = en(nc.psum_tensor("pt0", [128, 1024], BF16))
        c.pt1 = en(nc.psum_tensor("pt1", [128, 1024], BF16))

        for i, v in enumerate((LN_EPS, TINY, LOG_SCALE)):
            nc.const_aps.register(F32, v, pk.sl(c.sm, "consts", c0=i, ncols=1))

        P = Prog(nc)
        c.P = P
        c.s = {}
        for nm in ("ld", "cast", "st", "pe", "dve", "act", "gp"):
            c.s[nm] = Sem(ctx.enter_context(nc.semaphore("s_" + nm)), nm)
        P.auto = {"vector": c.s["dve"], "scalar": c.s["act"]}
        c.ccsems = [Sem(ctx.enter_context(nc.semaphore(f"s_cc{i}")), f"cc{i}")
                    for i in range(14)]
        c.cc_i = 0

        build_program(c, blocks)
        if taps:
            barrier(c)
            for nm in taps:
                obj = getattr(c, nm)
                if not isinstance(obj, bass.AP):
                    shp = list(obj.shape)
                    dt_ = obj.dtype
                    dst = nc.dram_tensor("tap_" + nm, shp, dt_, kind="ExternalOutput").ap()
                    P.on("sync", lambda e, o=obj, d=dst: e.dma_start(out=d, in_=o[:, :]),
                         inc=c.s["st"], inc_n=16)
                else:
                    shp = list(obj.tensor.shape)
                    dt_ = obj.tensor.dtype
                    dst = nc.dram_tensor("tap_" + nm, shp, dt_, kind="ExternalOutput").ap()
                    stg = c.arena[:, 0:shp[1]] if dt_ == BF16 else c.x_sb[:, 0:shp[1]]
                    for i0 in range(0, shp[0], 128):
                        i1 = min(i0 + 128, shp[0])
                        P.on("sync", lambda e, o=obj, i0=i0, i1=i1, stg=stg: e.dma_start(
                            out=stg[0:i1 - i0, :], in_=o[i0:i1, :]),
                             waits=[(c.s["st"], c.s["st"].n)], inc=c.s["ld"], inc_n=16)
                        ldv = c.s["ld"].n
                        P.on("sync", lambda e, d=dst, i0=i0, i1=i1, stg=stg: e.dma_start(
                            out=d[i0:i1, :], in_=stg[0:i1 - i0, :]),
                             waits=[(c.s["ld"], ldv)], inc=c.s["st"], inc_n=16)
            stv = c.s["st"].n
            P.on("sync", lambda e: None, waits=[(c.s["st"], stv)])

        with nc.Block() as block:
            P.run(block)
    return c


def barrier(c):
    P, s = c.P, c.s
    for eng, nm in (("tensor", "pe"), ("vector", "dve"), ("scalar", "act"),
                    ("gpsimd", "gp")):
        ops = P.ops[eng]
        if ops and ops[-1][2] is None:
            P.mark(eng, s[nm])
        elif not ops or ops[-1][2] is not s[nm]:
            P.on(eng, lambda e: e.nop(), inc=s[nm])
    snap = [(s[nm], s[nm].n) for nm in ("ld", "cast", "st", "pe", "dve", "act", "gp")]
    for eng in ENGINES:
        P.on(eng, lambda e: None, waits=snap)


def ones_row(c):
    return c.pkb.sl(c.smb, "tri", p0=0, p1=1)


def ones_1x1(c):
    return c.pkb.sl(c.smb, "tri", c0=127, ncols=1, p0=0, p1=1)


def ones_col2(c):
    return c.pkb.sl(c.smb, "tri", c0=127, ncols=1, p0=0, p1=2)


def ones_col128(c):
    return c.pkb.sl(c.smb, "tri", c0=127, ncols=1)


def load_row(c, name, ncols, c0=0, dst_c0=0):
    P, s = c.P, c.s
    off, _k = c.pr.slices[name]
    P.on("gpsimd", lambda e: e.dma_start(
        out=c.row_st[0:1, dst_c0:dst_c0 + ncols],
        in_=c.rows_d[0:1, off + c0:off + c0 + ncols]),
         waits=[(s["pe"], s["pe"].n)],
         inc=s["cast"], inc_n=16)
    return s["cast"].n


def cc_next(c):
    sem = c.ccsems[c.cc_i]
    c.cc_i += 1
    return sem


def build_program(c, blocks):
    P, s = c.P, c.s

    P.on("sync", lambda e: e.dma_start(out=c.sm[:], in_=c.smalls_d), inc=s["ld"], inc_n=16)
    P.on("gpsimd", lambda e: e.dma_start(out=c.smb[:], in_=c.smalls_b_d),
         inc=s["cast"], inc_n=16)
    P.on("gpsimd", lambda e: e.dma_start(out=c.selb[:], in_=c.sel_d), inc=s["cast"], inc_n=16)
    x_src = c.x_in.rearrange("(t p) d -> p t d", p=128)
    x_sb3 = c.x_sb[:].rearrange("p (t d) -> p t d", d=D)
    P.on("sync", lambda e: e.dma_start(out=x_sb3, in_=x_src), inc=s["ld"], inc_n=16)
    P.on("vector", lambda e: e.memset(c.f32a[:, :], 0.0))
    P.on("vector", lambda e: e.memset(c.row_f[:, :], 0.0))
    P.on("vector", lambda e: e.memset(c.stat[:, :], 0.0), inc=s["dve"])
    snap = [(s["ld"], s["ld"].n), (s["cast"], s["cast"].n), (s["dve"], s["dve"].n)]
    for eng in ENGINES:
        P.on(eng, lambda e: None, waits=snap)

    for b in blocks:
        if b == "l0_attn":
            emit_ln(c, dest="T")
            barrier(c)
            emit_attention(c, wkey="L0a", causal=False, l2norm=False,
                           bias_prefix="L0a", bo_name="L0a_bo")
        elif b == "l0_ffn":
            emit_spill_x(c)
            emit_ln(c, dest="x")
            barrier(c)
            for l in range(2):
                emit_ln(c, affine=f"I{l}_ln1", dest="T")
                barrier(c)
                emit_attention(c, wkey=f"I{l}", causal=True, l2norm=False,
                               bias_prefix=f"I{l}", bo_name=f"I{l}_bo")
                barrier(c)
                emit_ln(c, affine=f"I{l}_ln2", dest="T")
                barrier(c)
                if l == 0:
                    emit_token_gating(c, l)
                else:
                    emit_seq_gating(c, l)
                barrier(c)
                emit_experts(c, kind="inner", l=l)
                barrier(c)
                emit_b2_term(c, b2col=l * D)
                barrier(c)
            emit_restore_x(c)
        elif b in ("dbg_iattn0", "dbg_iattn1"):
            l = int(b[-1])
            emit_spill_x(c)
            emit_ln(c, dest="x")
            barrier(c)
            emit_ln(c, affine=f"I{l}_ln1", dest="T")
            barrier(c)
            emit_attention(c, wkey=f"I{l}", causal=True, l2norm=False,
                           bias_prefix=f"I{l}", bo_name=f"I{l}_bo")
            barrier(c)
            emit_restore_x(c)
        elif b in ("dbg_moe0", "dbg_moe1"):
            l = int(b[-1])
            emit_spill_x(c)
            emit_ln(c, dest="x")
            barrier(c)
            emit_ln(c, affine=f"I{l}_ln2", dest="T")
            barrier(c)
            if l == 0:
                emit_token_gating(c, l)
            else:
                emit_seq_gating(c, l)
            barrier(c)
            emit_experts(c, kind="inner", l=l)
            barrier(c)
            emit_b2_term(c, b2col=l * D)
            barrier(c)
            emit_restore_x(c)
        elif b == "l1_attn":
            emit_ln(c, dest="T")
            barrier(c)
            emit_attention(c, wkey="L1a", causal=False, l2norm=True,
                           bias_prefix=None, bo_name="L1a_bo")
        elif b == "l1_ffn":
            emit_ln(c, dest="T", keep_stats=True)
            barrier(c)
            emit_switch_gating(c)
            barrier(c)
            emit_gx(c)
            barrier(c)
            emit_experts(c, kind="l1")
            barrier(c)
            emit_b2_term(c, b2col=2 * D)
        else:
            raise ValueError(b)
        barrier(c)

    x_dst = c.x_out.rearrange("(t p) d -> p t d", p=128)
    P.on("sync", lambda e: e.dma_start(
        out=x_dst, in_=c.x_sb[:].rearrange("p (t d) -> p t d", d=D)),
        inc=s["st"], inc_n=16)
    stv = s["st"].n
    P.on("sync", lambda e: None, waits=[(s["st"], stv)])


def emit_spill_x(c):
    P, s = c.P, c.s
    dst = c.xres.rearrange("(t p) d -> p t d", p=128)
    P.on("sync", lambda e: e.dma_start(
        out=dst, in_=c.x_sb[:].rearrange("p (t d) -> p t d", d=D)),
        inc=s["st"], inc_n=16)
    c.xres_done = s["st"].n


def emit_restore_x(c):
    P, s = c.P, c.s
    barrier(c)
    src = c.xres.rearrange("(t p) d -> p t d", p=128)
    slot_dve = [0, 0]
    for tt in range(NT):
        sl = c.f32a[:, (tt % 2) * 1024:(tt % 2) * 1024 + 1024]
        P.on("sync", lambda e, sl=sl, tt=tt: e.dma_start(
            out=sl, in_=src[:, tt, :]),
             waits=[(s["dve"], slot_dve[tt % 2]), (s["st"], getattr(c, "_dbg_st", 0))],
             inc=s["ld"], inc_n=16)
        ldv = s["ld"].n
        P.on("vector", lambda e, sl=sl, tt=tt: e.tensor_tensor(
            c.x_sb[:, tt * D:(tt + 1) * D], c.x_sb[:, tt * D:(tt + 1) * D], sl, ALU.add),
             waits=[(s["ld"], ldv)], inc=s["dve"])
        slot_dve[tt % 2] = s["dve"].n


def emit_ln(c, affine=None, dest="T", keep_stats=False):
    P, s = c.P, c.s
    prep_dve = 0
    if affine is not None:
        emit_affine_prep(c, affine)
        prep_dve = s["dve"].n
    ev_slot = [0, 0]
    lnscr_pe = [0]
    apw = {}
    for tt in range(NT):
        xt = c.x_sb[:, tt * D:(tt + 1) * D]
        i = tt if keep_stats else (tt % 4)
        rstd = c.stat[:, i:i + 1]
        nmr = c.stat[:, 32 + i:33 + i]
        m_sl = c.stat[:, 64 + (tt % 4):65 + (tt % 4)]
        v_sl = c.stat[:, 68 + (tt % 4):69 + (tt % 4)]
        su_sl = c.stat[:, 72 + (tt % 4):73 + (tt % 4)]
        sq_sl = c.stat[:, 76 + (tt % 4):77 + (tt % 4)]
        w0 = apw.get(tt % 4, 0)
        P.on("vector", lambda e, xt=xt, o=su_sl: e.reduce_sum(o, xt, axis=AX),
             waits=[(s["act"], w0)], inc=s["dve"])
        pwb = c.pw0 if tt % 2 == 0 else c.pw1
        P.on("scalar", lambda e, xt=xt, o=sq_sl, pwb=pwb: e.activation(
            pwb[:, 0:1024], xt, AF.Square, accum_out=o),
             waits=[(s["act"], w0), (s["dve"], prep_dve if tt < 2 else 0)],
             inc=s["act"])
        dv, av = s["dve"].n, s["act"].n
        P.on("vector", lambda e, o=m_sl, i_=su_sl: e.tensor_scalar(
            o, i_, 1.0 / D, None, ALU.mult), waits=[(s["dve"], dv)])
        P.on("vector", lambda e, o=v_sl, m=m_sl: e.tensor_tensor(o, m, m, ALU.mult))
        P.on("vector", lambda e, o=v_sl, q=sq_sl, v2=v_sl: e.scalar_tensor_tensor(
            o, q, 1.0 / D, v2, ALU.mult, ALU.subtract),
             waits=[(s["act"], av)], inc=s["dve"])
        dv2 = s["dve"].n
        P.on("scalar", lambda e, o=rstd, i_=v_sl: e.activation(o, i_, AF.Ln, bias=LN_EPS),
             waits=[(s["dve"], dv2)])
        P.on("scalar", lambda e, o=rstd: e.activation(o, o, AF.Exp, scale=-0.5),
             inc=s["act"])
        av2 = s["act"].n
        P.on("vector", lambda e, o=nmr, m=m_sl, r=rstd: e.scalar_tensor_tensor(
            o, m, -1.0, r, ALU.mult, ALU.mult), waits=[(s["act"], av2)], inc=s["dve"])
        dv3 = s["dve"].n

        if dest == "x":
            xw = getattr(c, "xres_done", 0)
            P.on("scalar", lambda e, xt=xt, r=rstd, b=nmr: e.activation(
                xt, xt, AF.Identity, bias=b, scale=r),
                 waits=[(s["dve"], dv3), (s["st"], xw)], inc=s["act"])
            apw[tt % 4] = s["act"].n
            continue
        P.on("scalar", lambda e, xt=xt, r=rstd, b=nmr: e.activation(
            c.ln_scr[:], xt, AF.Identity, bias=b, scale=r),
             waits=[(s["dve"], dv3), (s["pe"], lnscr_pe[0])], inc=s["act"])
        apw[tt % 4] = s["act"].n
        avA = s["act"].n
        if affine is not None:
            P.on("vector", lambda e: e.tensor_tensor(
                c.ln_scr[:], c.ln_scr[:], c.stage[:, 0:D], ALU.mult),
                 waits=[(s["act"], avA)])
            P.on("vector", lambda e: e.tensor_tensor(
                c.ln_scr[:], c.ln_scr[:], c.stage[:, D:2 * D], ALU.add), inc=s["dve"])
        else:
            P.on("vector", lambda e: e.engine_nop(), waits=[(s["act"], avA)], inc=s["dve"])
        dvA = s["dve"].n
        tile_transpose_to_T(c, tt, dvA, ev_slot, lnscr_pe)


def tile_transpose_to_T(c, tt, dvA, ev_slot, lnscr_pe, src=None, dst=None):
    P, s = c.P, c.s
    if src is None:
        src = c.ln_scr
    ptb = c.pt0 if tt % 2 == 0 else c.pt1
    for f in range(FC):
        P.on("tensor", lambda e, f=f, ptb=ptb, src=src: e.transpose(
            ptb[:, f * 128:(f + 1) * 128], src[:, f * 128:(f + 1) * 128],
            c.pkb.sl(c.smb, "ident")),
             waits=([(s["dve"], dvA), (s["dve"], ev_slot[tt % 2])] if f == 0 else []))
    P.mark("tensor", s["pe"])
    lnscr_pe[0] = s["pe"].n
    pev = s["pe"].n
    if dst is None:
        dst = c.xnT
    for f in range(FC):
        P.on("vector", lambda e, f=f, ptb=ptb, dst=dst, tt=tt: e.tensor_copy(
            dst[:, f * R + tt * 128: f * R + (tt + 1) * 128],
            ptb[:, f * 128:(f + 1) * 128]),
             waits=([(s["pe"], pev)] if f == 0 else []))
    P.on("vector", lambda e: e.engine_nop(), inc=s["dve"])
    ev_slot[tt % 2] = s["dve"].n


def emit_affine_prep(c, affine):
    P, s = c.P, c.s
    for which, base in (("w", 0), ("b", D)):
        cw = load_row(c, affine + which, D)
        P.on("tensor", lambda e, base=base: e.matmul(
            c.pw0[:, 0:512], ones_row(c), c.row_st[0:1, 0:512], start=True, stop=True),
             waits=[(s["cast"], cw), (s["dve"], s["dve"].n)])
        P.on("tensor", lambda e, base=base: e.matmul(
            c.pw0[:, 512:1024], ones_row(c), c.row_st[0:1, 512:1024],
            start=True, stop=True), inc=s["pe"])
        pev = s["pe"].n
        P.on("vector", lambda e, base=base: e.tensor_copy(
            c.stage[:, base:base + D], c.pw0[:, 0:1024]),
             waits=[(s["pe"], pev)], inc=s["dve"])


def emit_attention(c, wkey, causal, l2norm, bias_prefix, bo_name):
    P, s, pk, pkb = c.P, c.s, c.pk, c.pkb
    scale = 0.125
    wkn = wkey in ("L0a", "L1a")
    qT = c.arena[:, QT_OFF:QT_OFF + FC * R]

    def kslot(i):
        return c.arena[:, KT_OFF + (i % 2) * 4096: KT_OFF + (i % 2) * 4096 + 4096]

    def vslot(i):
        return c.arena[:, VG_OFF + (i % 2) * 4224: VG_OFF + (i % 2) * 4224 + 4224]

    def wslot(i):
        return c.arena[:, WS_OFF + (i % 2) * 8192: WS_OFF + (i % 2) * 8192 + 8192]

    def wname(suffix):
        return f"{wkey}_{suffix.capitalize() if wkn else suffix}"

    def load_wslab(key, slot, wait=()):
        for f in range(FC):
            P.on("gpsimd", lambda e, f=f, key=key, slot=slot: e.dma_start(
                out=wslot(slot)[:, f * D:(f + 1) * D], in_=c.wd[key][f]),
                 waits=(list(wait) if f == 0 else []), inc=s["cast"], inc_n=16)
        return s["cast"].n

    state = {"pj_ev": {0: 0, 1: 0}, "stg_st": 0, "nt_pe": 0, "nt_dve": 0,
             "nt_act": 0, "sq_slot": [0, 0]}

    def norm_track(kind, oc, chunk):
        col = oc if kind == "k" else 8 + oc
        for n in range(4):
            sqs = c.ln_scr[:, (n % 2) * 512:(n % 2) * 512 + 512]
            P.on("scalar", lambda e, chunk=chunk, n=n, sqs=sqs: e.activation(
                sqs, chunk[:, n * 512:(n + 1) * 512], AF.Square),
                 waits=[(s["pe"], state["sq_slot"][n % 2]),
                        (s["dve"], state["nt_dve"])], inc=s["act"])
            sqv = s["act"].n
            P.on("tensor", lambda e, sqs=sqs: e.matmul(
                c.pw1[0:2, 0:512], pkb.sl(c.smb, "onespair"), sqs,
                start=True, stop=True),
                 waits=[(s["act"], sqv), (s["dve"], state["nt_dve"])], inc=s["pe"])
            state["sq_slot"][n % 2] = s["pe"].n
            pev = s["pe"].n
            P.on("vector", lambda e, n=n: e.reduce_max(
                c.row_f[0:2, 32 + n:33 + n], c.pw1[0:2, 0:512], axis=AX),
                 waits=[(s["pe"], pev)], inc=s["dve"])
            state["nt_dve"] = s["dve"].n
        P.on("vector", lambda e, col=col: e.reduce_max(
            c.row_f[0:2, col:col + 1], c.row_f[0:2, 32:36], axis=AX), inc=s["dve"])
        state["nt_dve"] = s["dve"].n

    def l2norm_apply(kind, oc, chunk):
        ssl = pk.sl(c.sm, "L1_ks" if kind == "k" else "L1_qs", c0=oc, ncols=1)
        for n in range(4):
            sqs = c.ln_scr[:, (n % 2) * 512:(n % 2) * 512 + 512]
            P.on("scalar", lambda e, chunk=chunk, n=n, sqs=sqs: e.activation(
                sqs, chunk[:, n * 512:(n + 1) * 512], AF.Square),
                 waits=[(s["pe"], state["sq_slot"][n % 2])], inc=s["act"])
            sqv = s["act"].n
            P.on("tensor", lambda e, sqs=sqs: e.matmul(
                c.pw1[0:2, 0:512], pkb.sl(c.smb, "onespair"), sqs,
                start=True, stop=True),
                 waits=[(s["act"], sqv), (s["dve"], state["nt_dve"])], inc=s["pe"])
            pev = s["pe"].n
            rln = c.f32a[0:2, 1024 + (n % 2) * 512:1024 + (n % 2) * 512 + 512]
            P.on("vector", lambda e, rln=rln: e.tensor_scalar(
                rln, c.pw1[0:2, 0:512], 1e-24, None, ALU.max),
                 waits=[(s["pe"], pev)], inc=s["dve"])
            state["nt_dve"] = s["dve"].n
            dvv = s["dve"].n
            P.on("scalar", lambda e, rln=rln: e.activation(rln, rln, AF.Ln),
                 waits=[(s["dve"], dvv)])
            P.on("scalar", lambda e, rln=rln: e.activation(rln, rln, AF.Exp, scale=-0.5),
                 inc=s["act"])
            rlv = s["act"].n
            P.on("tensor", lambda e, rln=rln: e.matmul(
                c.pw0[:, 0:512], pk.sl(c.sm, "bcpairT_f", p0=0, p1=2), rln,
                start=True, stop=True),
                 waits=[(s["act"], rlv), (s["dve"], state["nt_dve"])], inc=s["pe"])
            state["sq_slot"][n % 2] = s["pe"].n
            pev2 = s["pe"].n
            P.on("vector", lambda e, chunk=chunk, n=n, ssl=ssl: e.scalar_tensor_tensor(
                chunk[:, n * 512:(n + 1) * 512], chunk[:, n * 512:(n + 1) * 512],
                ssl, c.pw0[:, 0:512], ALU.mult, ALU.mult),
                 waits=[(s["pe"], pev2)], inc=s["dve"])
            state["nt_dve"] = s["dve"].n

    def proj(kind, slot, w_done, bias_name, out_mode):
        for oc in range(FC):
            for n in range(4):
                bank = c.pa if (oc * 4 + n) % 2 == 0 else c.pb
                bk_i = (oc * 4 + n) % 2
                for f in range(FC):
                    P.on("tensor", lambda e, f=f, oc=oc, n=n, bank=bank, slot=slot: e.matmul(
                        bank[:], wslot(slot)[:, f * D + oc * 128:f * D + (oc + 1) * 128],
                        c.xnT[:, f * R + n * 512: f * R + (n + 1) * 512],
                        start=(f == 0), stop=(f == FC - 1)),
                         waits=([(s["cast"], w_done), (s["act"], state["pj_ev"][bk_i])]
                                if f == 0 else []))
                P.mark("tensor", s["pe"])
                pev = s["pe"].n
                if out_mode == "cc":
                    dst = c.stage[:, n * 512:(n + 1) * 512]
                    extra = [(s["st"], state["stg_st"])]
                else:
                    dst = qT[:, oc * R + n * 512: oc * R + (n + 1) * 512]
                    extra = []
                if bias_name is not None:
                    bsl = pk.sl(c.sm, bias_name, c0=oc, ncols=1)
                    P.on("scalar", lambda e, dst=dst, bank=bank, bsl=bsl: e.activation(
                        dst, bank[:], AF.Identity, bias=bsl, scale=1.0),
                         waits=[(s["pe"], pev)] + extra, inc=s["act"])
                else:
                    P.on("scalar", lambda e, dst=dst, bank=bank: e.activation(
                        dst, bank[:], AF.Identity),
                         waits=[(s["pe"], pev)] + extra, inc=s["act"])
                state["pj_ev"][bk_i] = s["act"].n
            chunk = c.stage[:, 0:2048] if out_mode == "cc" else qT[:, oc * R:(oc + 1) * R]
            if l2norm and kind in ("k", "q"):
                l2norm_apply(kind, oc, chunk)
            if kind in ("k", "q"):
                norm_track(kind, oc, chunk)
            if out_mode == "cc":
                cc = c.k_cc_in if kind == "k" else c.v_cc_in
                P.on("sync", lambda e, oc=oc, cc=cc: e.dma_start(
                    out=cc[oc * 128:(oc + 1) * 128, :], in_=c.stage[:, 0:2048]),
                     waits=[(s["act"], s["act"].n), (s["dve"], s["dve"].n)],
                     inc=s["st"], inc_n=16)
                state["stg_st"] = s["st"].n

    bq = f"{bias_prefix}_bq" if bias_prefix else None
    bk = f"{bias_prefix}_bk" if bias_prefix else None
    bv = f"{bias_prefix}_bv" if bias_prefix else None

    wk_done = load_wslab(wname("wk"), 0)
    wv_done = load_wslab(wname("wv"), 1)
    proj("k", 0, wk_done, bk, "cc")
    k_st = s["st"].n
    ccK = cc_next(c)
    P.on("gpsimd", lambda e: e.collective_compute(
        "AllGather", ALU.bypass, replica_groups=PAIRS,
        ins=[c.k_cc_in.opt()], outs=[c.k_cc_out.opt()]),
         waits=[(s["st"], k_st)], inc=ccK)
    wq_done = load_wslab(wname("wq"), 0, wait=[(s["pe"], s["pe"].n)])
    proj("v", 1, wv_done, bv, "cc")
    v_st = s["st"].n
    ccV = cc_next(c)
    P.on("gpsimd", lambda e: e.collective_compute(
        "AllGather", ALU.bypass, replica_groups=PAIRS,
        ins=[c.v_cc_in.opt()], outs=[c.v_cc_out.opt()]),
         waits=[(s["st"], v_st)], inc=ccV)
    proj("q", 0, wq_done, bq, "qT")

    dvk = s["dve"].n
    P.on("sync", lambda e: e.dma_start(out=c.nrm_cc_in, in_=c.row_f[0:2, 0:8]),
         waits=[(s["dve"], dvk)], inc=s["st"], inc_n=16)
    nst = s["st"].n
    ccN = cc_next(c)
    P.on("gpsimd", lambda e: e.collective_compute(
        "AllReduce", ALU.max, replica_groups=PAIRS,
        ins=[c.nrm_cc_in.opt()], outs=[c.nrm_cc_out.opt()]),
         waits=[(s["st"], nst)], inc=ccN)
    P.on("sync", lambda e: e.dma_start(out=c.row_f[0:2, 16:24], in_=c.nrm_cc_out),
         waits=[(ccN, 1)], inc=s["ld"], inc_n=16)
    nld = s["ld"].n
    P.on("scalar", lambda e: e.activation(c.row_fb[0:2, 0:8], c.row_f[0:2, 16:24], AF.Copy),
         waits=[(s["ld"], nld)])
    P.on("scalar", lambda e: e.activation(c.row_fb[0:2, 8:16], c.row_f[0:2, 8:16], AF.Copy),
         inc=s["act"])
    acv = s["act"].n
    P.on("tensor", lambda e: e.matmul(c.pa[0:1, 0:16], ones_col2(c),
                                      c.row_fb[0:2, 0:16], start=True, stop=True),
         waits=[(s["act"], acv)], inc=s["pe"])
    pev = s["pe"].n
    P.on("vector", lambda e: e.tensor_copy(c.row_f[0:1, 32:48], c.pa[0:1, 0:16]),
         waits=[(s["pe"], pev)], inc=s["dve"])
    dvv = s["dve"].n
    P.on("scalar", lambda e: e.activation(c.row_f[0:1, 32:48], c.row_f[0:1, 32:48],
                                          AF.Ln, bias=TINY),
         waits=[(s["dve"], dvv)], inc=s["act"])
    acv2 = s["act"].n
    P.on("vector", lambda e: e.tensor_tensor(
        c.row_f[0:1, 48:56], c.row_f[0:1, 32:40], c.row_f[0:1, 40:48], ALU.add),
         waits=[(s["act"], acv2)], inc=s["dve"])
    dvv2 = s["dve"].n
    P.on("scalar", lambda e: e.activation(c.row_f[0:1, 48:56], c.row_f[0:1, 48:56],
                                          AF.Exp, scale=0.5,
                                          bias=LOG_SCALE),
         waits=[(s["dve"], dvv2)], inc=s["act"])
    acv3 = s["act"].n
    P.on("vector", lambda e: e.tensor_scalar(c.row_f[0:1, 48:56], c.row_f[0:1, 48:56],
                                             -60.0, None, ALU.add),
         waits=[(s["act"], acv3)], inc=s["dve"])
    dvv2b = s["dve"].n
    P.on("scalar", lambda e: e.activation(c.row_f[0:1, 48:56], c.row_f[0:1, 48:56],
                                          AF.Relu),
         waits=[(s["dve"], dvv2b)], inc=s["act"])
    acv3b = s["act"].n
    P.on("vector", lambda e: e.tensor_scalar(c.row_f[0:1, 48:56], c.row_f[0:1, 48:56],
                                             -1.0, None, ALU.mult),
         waits=[(s["act"], acv3b)], inc=s["dve"])
    dvv3 = s["dve"].n
    P.on("scalar", lambda e: e.activation(c.row_fb[0:1, 16:24], c.row_f[0:1, 48:56],
                                          AF.Copy), waits=[(s["dve"], dvv3)], inc=s["act"])
    acv4 = s["act"].n
    P.on("tensor", lambda e: e.matmul(c.pa[:, 0:8], ones_row(c), c.row_fb[0:1, 16:24],
                                      start=True, stop=True),
         waits=[(s["act"], acv4)], inc=s["pe"])
    pev2 = s["pe"].n
    bias_base = c.f32a[:, 2048:2056]
    bias_mid = c.f32a[:, 2056:2064]
    bias_b1 = c.f32a[:, 2064:2072]
    P.on("vector", lambda e: e.tensor_copy(bias_base, c.pa[:, 0:8]),
         waits=[(s["pe"], pev2)], inc=s["dve"])
    bias_dve = s["dve"].n
    if causal:
        cro = load_row(c, "coff", 2)
        P.on("tensor", lambda e: e.matmul(c.pb[:, 0:2], ones_row(c), c.row_st[0:1, 0:2],
                                          start=True, stop=True),
             waits=[(s["cast"], cro), (s["dve"], bias_dve)], inc=s["pe"])
        pev3 = s["pe"].n
        P.on("vector", lambda e: e.tensor_scalar(
            bias_mid, bias_base, c.pb[:, 0:1], None, ALU.add),
             waits=[(s["pe"], pev3)])
        P.on("vector", lambda e: e.tensor_scalar(
            bias_b1, bias_base, c.pb[:, 1:2], None, ALU.add), inc=s["dve"])
        bias_dve = s["dve"].n

    def build_vg(g, wait_pe):
        ks, vs = kslot(g), vslot(g)
        P.on("sync", lambda e, g=g, ks=ks: e.dma_start(
            out=ks[:, 0:2048], in_=c.v_cc_out[g * 128:(g + 1) * 128, :]),
             waits=[(ccV, 1), (s["pe"], wait_pe)], inc=s["ld"], inc_n=16)
        P.on("sync", lambda e, g=g, ks=ks: e.dma_start(
            out=ks[:, 2048:4096], in_=c.v_cc_out[D + g * 128:D + (g + 1) * 128, :]),
             inc=s["ld"], inc_n=16)
        vld = s["ld"].n
        P.on("vector", lambda e, vs=vs: e.memset(vs, 1.0),
             waits=[(s["pe"], wait_pe)], inc=s["dve"])
        msv = s["dve"].n
        for rnd in range(4):
            for k in range(8):
                kc = rnd * 8 + k
                P.on("tensor", lambda e, kc=kc, k=k, ks=ks: e.transpose(
                    c.pt1[:, k * 128:(k + 1) * 128], ks[:, kc * 128:(kc + 1) * 128],
                    pkb.sl(c.smb, "ident")),
                     waits=([(s["ld"], vld), (s["dve"], msv),
                             (s["dve"], state.get("vg_ev", 0))] if k == 0 else []))
            P.mark("tensor", s["pe"])
            pev = s["pe"].n
            for k in range(8):
                kc = rnd * 8 + k
                P.on("vector", lambda e, kc=kc, k=k, vs=vs: e.tensor_copy(
                    vs[:, kc * 132:kc * 132 + 64], c.pt1[:, k * 128:k * 128 + 64]),
                     waits=([(s["pe"], pev)] if k == 0 else []))
                P.on("vector", lambda e, kc=kc, k=k, vs=vs: e.tensor_copy(
                    vs[:, kc * 132 + 66:kc * 132 + 130],
                    c.pt1[:, k * 128 + 64:(k + 1) * 128]))
            P.on("vector", lambda e: e.engine_nop(), inc=s["dve"])
            state["vg_ev"] = s["dve"].n
        vgd = s["dve"].n
        P.on("sync", lambda e, g=g, ks=ks: e.dma_start(
            out=ks[:, 0:2048], in_=c.k_cc_out[g * 128:(g + 1) * 128, :]),
             waits=[(ccK, 1), (s["dve"], vgd)], inc=s["ld"], inc_n=16)
        P.on("sync", lambda e, g=g, ks=ks: e.dma_start(
            out=ks[:, 2048:4096], in_=c.k_cc_out[D + g * 128:D + (g + 1) * 128, :]),
             inc=s["ld"], inc_n=16)
        return s["ld"].n, vgd

    def kc_list(qs):
        if not causal:
            return [(kc, "full") for kc in range(32)]
        out = []
        for kc in range(16 + qs * 4 + 4):
            if kc < qs * 4:
                out.append((kc, "full"))
            elif kc < qs * 4 + 4:
                out.append((kc, "band0"))
            elif kc < 16 + qs * 4:
                out.append((kc, "mid"))
            else:
                out.append((kc, "band1"))
        return out

    kq = {0: build_vg(0, s["pe"].n)}
    exp_slot = [0, 0]
    mask_slot = [0, 0]
    av_pe_slot = [0, 0]
    av_done = [0]
    rl_state = {"act": 0, "ts": 0}
    for g in range(FC):
        kld, vgd = kq[g]
        if g + 1 < FC:
            kq[g + 1] = build_vg(g + 1, s["pe"].n)
        for qs in range(4):
            kcl = kc_list(qs)
            first_kc, last_kc = kcl[0][0], kcl[-1][0]
            for kc, band in kcl:
                pw = c.pw0 if kc % 2 == 0 else c.pw1
                bi = kc % 2
                P.on("tensor", lambda e, pw=pw, kc=kc, qs=qs, g=g: e.matmul(
                    pw[:, 0:512], kslot(g)[0:64, kc * 128:(kc + 1) * 128],
                    qT[0:64, g * R + qs * 512:g * R + (qs + 1) * 512],
                    start=True, stop=True, tile_position=(0, 0)),
                     waits=[(s["act"], exp_slot[bi]), (s["ld"], kld)])
                P.on("tensor", lambda e, pw=pw, kc=kc, qs=qs, g=g: e.matmul(
                    pw[:, 512:1024], kslot(g)[64:128, kc * 128:(kc + 1) * 128],
                    qT[64:128, g * R + qs * 512:g * R + (qs + 1) * 512],
                    start=True, stop=True, tile_position=(64, 0)))
                P.mark("tensor", s["pe"])
                spe = s["pe"].n
                bias = bias_base if band in ("full", "band0") else (
                    bias_mid if band == "mid" else bias_b1)
                prng = c.stage[:, bi * 1024:(bi + 1) * 1024]
                P.on("scalar", lambda e, pw=pw, prng=prng, bias=bias, g=g: e.activation(
                    prng, pw[:, 0:1024], AF.Exp, scale=scale, bias=bias[:, g:g + 1]),
                     waits=[(s["pe"], spe), (s["dve"], bias_dve),
                            (s["pe"], av_pe_slot[bi]), (s["dve"], mask_slot[bi])],
                     inc=s["act"])
                exp_slot[bi] = s["act"].n
                mask_dve = 0
                if causal and band in ("band0", "band1"):
                    mnm = "mb0" if band == "band0" else "mb1"
                    o = kc - (qs * 4 if band == "band0" else 16 + qs * 4)
                    msl = pkb.sl(c.smb, mnm, c0=o * 512, ncols=512)
                    P.on("vector", lambda e, prng=prng, msl=msl: e.tensor_tensor(
                        prng[:, 0:512], prng[:, 0:512], msl, ALU.mult),
                         waits=[(s["act"], exp_slot[bi])])
                    P.on("vector", lambda e, prng=prng, msl=msl: e.tensor_tensor(
                        prng[:, 512:1024], prng[:, 512:1024], msl, ALU.mult),
                         inc=s["dve"])
                    mask_dve = s["dve"].n
                    mask_slot[bi] = mask_dve
                wts2 = [(s["act"], exp_slot[bi])]
                if mask_dve:
                    wts2.append((s["dve"], mask_dve))
                if kc == first_kc:
                    wts2 += [(s["dve"], av_done[0]), (s["dve"], vgd)]
                P.on("tensor", lambda e, prng=prng, kc=kc, g=g,
                     st=(kc == first_kc), sp=(kc == last_kc): e.matmul(
                    c.pa[0:65, :], vslot(g)[:, kc * 132:kc * 132 + 65],
                    prng[:, 0:512], start=st, stop=sp), waits=wts2)
                P.on("tensor", lambda e, prng=prng, kc=kc, g=g,
                     st=(kc == first_kc), sp=(kc == last_kc): e.matmul(
                    c.pb[0:65, :], vslot(g)[:, kc * 132 + 66:kc * 132 + 131],
                    prng[:, 512:1024], start=st, stop=sp))
                P.mark("tensor", s["pe"])
                av_pe_slot[bi] = s["pe"].n
            avp = s["pe"].n
            P.on("vector", lambda e: e.tensor_copy(c.ln_scr[0:65, 0:512], c.pa[0:65, :]),
                 waits=[(s["pe"], avp), (s["act"], rl_state["act"]),
                        (s["dve"], rl_state["ts"])])
            P.on("vector", lambda e: e.tensor_copy(c.ln_scr[0:65, 512:1024], c.pb[0:65, :]),
                 inc=s["dve"])
            av_done[0] = s["dve"].n
            evd = s["dve"].n
            for k8 in range(8):
                P.on("tensor", lambda e, k8=k8: e.transpose(
                    c.pt0[0:128, k8 * 66:k8 * 66 + 65],
                    c.ln_scr[0:65, k8 * 128:(k8 + 1) * 128],
                    pkb.sl(c.smb, "ident", ncols=65, p0=0, p1=65)),
                     waits=([(s["dve"], evd), (s["act"], rl_state["act"]),
                             (s["dve"], rl_state["ts"])] if k8 == 0 else []))
            P.mark("tensor", s["pe"])
            trp = s["pe"].n
            lcols = c.pt0[:, 64:527:66]
            rl = c.f32a[:, 2072:2080]
            P.on("scalar", lambda e, lc=lcols, rl=rl: e.activation(rl, lc, AF.Ln,
                                                                  bias=TINY),
                 waits=[(s["pe"], trp)])
            P.on("scalar", lambda e, rl=rl: e.activation(rl, rl, AF.Exp, scale=-1.0),
                 inc=s["act"])
            rl_state["act"] = s["act"].n
            rlv = s["act"].n
            for k8 in range(8):
                hh, k = k8 // 4, k8 % 4
                dst = qT[:, g * R + qs * 512 + k * 128 + hh * 64:
                         g * R + qs * 512 + k * 128 + hh * 64 + 64]
                P.on("vector", lambda e, k8=k8, dst=dst, rl=rl: e.tensor_scalar(
                    dst, c.pt0[0:128, k8 * 66:k8 * 66 + 64], rl[:, k8:k8 + 1],
                    None, ALU.mult),
                     waits=([(s["act"], rlv)] if k8 == 0 else []))
            P.on("vector", lambda e: e.engine_nop(), inc=s["dve"])
            rl_state["ts"] = s["dve"].n

    barrier(c)
    ev_slot = [0, 0]
    lnscr_pe = [0]
    for tt in range(NT):
        ptb = c.pt0 if tt % 2 == 0 else c.pt1
        qsl, k = tt // 4, tt % 4
        for f in range(FC):
            src_ap = qT[:, f * R + qsl * 512 + k * 128: f * R + qsl * 512 + (k + 1) * 128]
            P.on("tensor", lambda e, f=f, ptb=ptb, src_ap=src_ap: e.transpose(
                ptb[:, f * 128:(f + 1) * 128], src_ap, pkb.sl(c.smb, "ident")),
                 waits=([(s["dve"], ev_slot[tt % 2])] if f == 0 else []))
        P.mark("tensor", s["pe"])
        pev = s["pe"].n
        for f in range(FC):
            P.on("vector", lambda e, f=f, ptb=ptb, tt=tt: e.tensor_copy(
                c.xnT[:, f * R + tt * 128: f * R + (tt + 1) * 128],
                ptb[:, f * 128:(f + 1) * 128]),
                 waits=([(s["pe"], pev)] if f == 0 else []))
        P.on("vector", lambda e: e.engine_nop(), inc=s["dve"])
        ev_slot[tt % 2] = s["dve"].n
    aT_dve = s["dve"].n

    for f in range(FC):
        P.on("gpsimd", lambda e, f=f: e.dma_start(
            out=c.arena[:, KT_OFF + f * D: KT_OFF + (f + 1) * D],
            in_=c.wd[wname("wo")][f]),
             waits=([(s["pe"], s["pe"].n)] if f == 0 else []), inc=s["cast"], inc_n=16)
    wo_done = s["cast"].n
    bo_cast = load_row(c, bo_name, D)
    woslab = c.arena[:, KT_OFF:KT_OFF + 8192]
    res_slot = [0, 0]
    for tc in range(NT):
        for no in range(2):
            bank = c.pa if (tc * 2 + no) % 2 == 0 else c.pb
            for f in range(FC):
                P.on("tensor", lambda e, f=f, tc=tc, no=no, bank=bank: e.matmul(
                    bank[:], c.xnT[:, f * R + tc * 128:f * R + (tc + 1) * 128],
                    woslab[:, f * D + no * 512: f * D + (no + 1) * 512],
                    start=(f == 0), stop=False),
                     waits=([(s["cast"], wo_done), (s["dve"], aT_dve),
                             (s["dve"], res_slot[(tc * 2 + no) % 2])] if f == 0 else []))
            P.on("tensor", lambda e, no=no, bank=bank: e.matmul(
                bank[:], ones_row(c), c.row_st[0:1, no * 512:(no + 1) * 512],
                start=False, stop=True),
                 waits=[(s["cast"], bo_cast)])
            P.mark("tensor", s["pe"])
            pev = s["pe"].n
            xsl = c.x_sb[:, tc * D + no * 512: tc * D + (no + 1) * 512]
            P.on("vector", lambda e, xsl=xsl, bank=bank: e.tensor_tensor(
                xsl, xsl, bank[:], ALU.add), waits=[(s["pe"], pev)], inc=s["dve"])
            res_slot[(tc * 2 + no) % 2] = s["dve"].n


def emit_excl_mask(c, src_ap, mx, eq, pfA, pfB, m_out, p1=128):
    P = c.P
    P.on("vector", lambda e: e.reduce_max(mx, src_ap, axis=AX))
    P.on("vector", lambda e: e.tensor_scalar(eq, src_ap, mx, None, ALU.is_equal))
    P.on("vector", lambda e: e.tensor_copy(pfA[:, 0:1], eq[:, 0:1]))
    P.on("vector", lambda e: e.tensor_tensor(pfA[:, 1:8], eq[:, 1:8], eq[:, 0:7], ALU.add))
    P.on("vector", lambda e: e.tensor_copy(pfB[:, 0:2], pfA[:, 0:2]))
    P.on("vector", lambda e: e.tensor_tensor(pfB[:, 2:8], pfA[:, 2:8], pfA[:, 0:6], ALU.add))
    P.on("vector", lambda e: e.tensor_copy(pfA[:, 0:4], pfB[:, 0:4]))
    P.on("vector", lambda e: e.tensor_tensor(pfA[:, 4:8], pfB[:, 4:8], pfB[:, 0:4], ALU.add))
    P.on("vector", lambda e: e.tensor_tensor(pfA, pfA, eq, ALU.subtract))
    P.on("vector", lambda e: e.tensor_scalar(pfA, pfA, 0.0, None, ALU.is_equal))
    P.on("vector", lambda e: e.tensor_tensor(m_out, eq, pfA, ALU.mult))


def top2_tile(c, lg_ap, cw_dst, prt=128, scr0=640):
    P, s = c.P, c.s
    p1 = prt
    sc = lambda a, b: c.f32a[0:p1, scr0 + a:scr0 + b]
    mx1, mx2 = sc(0, 1), sc(1, 2)
    eq = sc(2, 10)
    pfA = sc(10, 18)
    pfB = sc(44, 52)
    m1 = sc(18, 26)
    m2 = sc(26, 34)
    w1, w2, t1 = sc(34, 35), sc(35, 36), sc(36, 44)

    def excl_mask(src_ap, mx, m_out):
        emit_excl_mask(c, src_ap, mx, eq, pfA, pfB, m_out, p1=p1)

    excl_mask(lg_ap, mx1, m1)
    P.on("vector", lambda e: e.scalar_tensor_tensor(t1, m1, NEG_BIG, lg_ap,
                                                    ALU.mult, ALU.add))
    excl_mask(t1, mx2, m2)
    P.on("vector", lambda e: e.tensor_scalar(w2, mx2, -1.0, None, ALU.mult), inc=s["dve"])
    dv = s["dve"].n
    P.on("scalar", lambda e: e.activation(w1, mx1, AF.Sigmoid, bias=w2, scale=1.0),
         waits=[(s["dve"], dv)], inc=s["act"])
    av = s["act"].n
    P.on("vector", lambda e: e.tensor_scalar(t1[:, 0:1], mx1, -1.0, None, ALU.mult),
         waits=[(s["act"], av)], inc=s["dve"])
    dv2 = s["dve"].n
    P.on("scalar", lambda e: e.activation(w2, mx2, AF.Sigmoid, bias=t1[:, 0:1], scale=1.0),
         waits=[(s["dve"], dv2)], inc=s["act"])
    av2 = s["act"].n
    P.on("vector", lambda e: e.tensor_scalar(t1, m1, w1, None, ALU.mult),
         waits=[(s["act"], av2)])
    P.on("vector", lambda e: e.scalar_tensor_tensor(cw_dst, m2, w2, t1, ALU.mult, ALU.add),
         inc=s["dve"])


def logits_tokmaj(c, gw_name, gb_row_name, extra_done=()):
    P, s = c.P, c.s
    gb_cast = load_row(c, gb_row_name, 8)
    banks = [c.pa, c.pb]
    for tc in range(NT):
        bank = banks[tc % 2]
        for f in range(FC):
            P.on("tensor", lambda e, f=f, tc=tc, bank=bank: e.matmul(
                bank[:, 0:8], c.xnT[:, f * R + tc * 128:f * R + (tc + 1) * 128],
                c.pkb.sl(c.smb, gw_name, c0=f * E, ncols=E),
                start=(f == 0), stop=False),
                 waits=(list(extra_done) + [(s["dve"], c._lg_free.get(tc % 2, 0))]
                        if f == 0 else []))
        P.on("tensor", lambda e, bank=bank: e.matmul(
            bank[:, 0:8], ones_row(c), c.row_st[0:1, 0:8], start=False, stop=True),
             waits=[(s["cast"], gb_cast)])
        P.mark("tensor", s["pe"])
        yield tc, bank, s["pe"].n


def emit_token_gating(c, l):
    P, s = c.P, c.s
    c._lg_free = {}
    for tc, bank, pev in logits_tokmaj(c, f"I{l}_gw", f"gb{l}"):
        P.on("vector", lambda e: e.engine_nop(), waits=[(s["pe"], pev)])
        cw = c.f32a[:, tc * 8:(tc + 1) * 8]
        top2_tile(c, bank[:, 0:8], cw)
        c._lg_free[tc % 2] = s["dve"].n


def emit_seq_gating(c, l):
    P, s = c.P, c.s
    mean = c.f32a[:, 512:512 + 8]
    for f in range(FC):
        P.on("vector", lambda e, f=f: e.reduce_sum(
            mean[:, f:f + 1], c.xnT[:, f * R:(f + 1) * R], axis=AX),
             inc=(s["dve"] if f == FC - 1 else None))
    dv = s["dve"].n
    P.on("sync", lambda e: e.dma_start(out=c.mean_cc_in, in_=mean),
         waits=[(s["dve"], dv)], inc=s["st"], inc_n=16)
    stv = s["st"].n
    ccM = cc_next(c)
    P.on("gpsimd", lambda e: e.collective_compute(
        "AllReduce", ALU.add, replica_groups=PAIRS,
        ins=[c.mean_cc_in.opt()], outs=[c.mean_cc_out.opt()]),
         waits=[(s["st"], stv)], inc=ccM)
    P.on("sync", lambda e: e.dma_start(out=mean, in_=c.mean_cc_out),
         waits=[(ccM, 1)], inc=s["ld"], inc_n=16)
    ldv = s["ld"].n
    P.on("vector", lambda e: e.tensor_scalar(mean, mean, 1.0 / SEQ, None, ALU.mult),
         waits=[(s["ld"], ldv)], inc=s["dve"])
    dv2 = s["dve"].n
    mb = c.ln_scr[:, 0:8]
    P.on("scalar", lambda e: e.activation(mb, mean, AF.Copy),
         waits=[(s["dve"], dv2)], inc=s["act"])
    av = s["act"].n
    gb_cast = load_row(c, f"gb{l}", 8)
    for f in range(FC):
        P.on("tensor", lambda e, f=f: e.matmul(
            c.pa[0:1, 0:8], mb[:, f:f + 1],
            c.pkb.sl(c.smb, f"I{l}_gw", c0=f * E, ncols=E),
            start=(f == 0), stop=False),
             waits=([(s["act"], av)] if f == 0 else []))
    P.on("tensor", lambda e: e.matmul(c.pa[0:1, 0:8], ones_1x1(c), c.row_st[0:1, 0:8],
                                      start=False, stop=True),
         waits=[(s["cast"], gb_cast)])
    P.mark("tensor", s["pe"])
    pev = s["pe"].n
    P.on("vector", lambda e: e.engine_nop(), waits=[(s["pe"], pev)])
    cwrow = c.f32a[0:1, 128:136]
    top2_tile(c, c.pa[0:1, 0:8], cwrow, prt=1)
    dv3 = s["dve"].n
    P.on("scalar", lambda e: e.activation(c.row_fb[0:1, 32:40], cwrow, AF.Copy),
         waits=[(s["dve"], dv3)], inc=s["act"])
    av2 = s["act"].n
    P.on("tensor", lambda e: e.matmul(c.pb[:, 0:8], ones_row(c), c.row_fb[0:1, 32:40],
                                      start=True, stop=True),
         waits=[(s["act"], av2)], inc=s["pe"])
    pev2 = s["pe"].n
    for tc in range(NT):
        P.on("vector", lambda e, tc=tc: e.tensor_copy(
            c.f32a[:, tc * 8:(tc + 1) * 8], c.pb[:, 0:8]),
             waits=([(s["pe"], pev2)] if tc == 0 else []))
    P.on("vector", lambda e: e.engine_nop(), inc=s["dve"])


def emit_switch_gating(c):
    P, s = c.P, c.s
    c._lg_free = {}
    c._sw_free = {}
    m1b_all = c.ln_scr
    for tc, bank, pev in logits_tokmaj(c, "L1_wr", "br"):
        P.on("vector", lambda e: e.engine_nop(), waits=[(s["pe"], pev)])
        sc = lambda a, b: c.f32a[:, 640 + a:640 + b]
        mx1, eq, pfA, pfB = sc(0, 1), sc(2, 10), sc(10, 18), sc(44, 52)
        m1 = c.f32a[:, 1024 + tc * 8:1024 + (tc + 1) * 8]
        emit_excl_mask(c, bank[:, 0:8], mx1, eq, pfA, pfB, m1)
        nmx = sc(1, 2)
        P.on("vector", lambda e: e.tensor_scalar(nmx, mx1, -1.0, None, ALU.mult),
             inc=s["dve"])
        dv = s["dve"].n
        lsum = c.f32a[:, 256 + tc:257 + tc]
        P.on("scalar", lambda e, bank=bank, lsum=lsum: e.activation(
            sc(10, 18), bank[:, 0:8], AF.Exp, bias=nmx, scale=1.0, accum_out=lsum),
             waits=[(s["dve"], dv)], inc=s["act"])
        av = s["act"].n
        P.on("scalar", lambda e, lsum=lsum: e.activation(lsum, lsum, AF.Ln),
             waits=[(s["act"], av)])
        P.on("scalar", lambda e, lsum=lsum: e.activation(lsum, lsum, AF.Exp, scale=-1.0),
             inc=s["act"])
        av2 = s["act"].n
        P.on("vector", lambda e, tc=tc, m1=m1: e.tensor_copy(
            m1b_all[:, tc * 8:(tc + 1) * 8], m1), waits=[(s["act"], av2)], inc=s["dve"])
        c._lg_free[tc % 2] = s["dve"].n
    m1_dve = s["dve"].n
    for tc in range(NT):
        P.on("tensor", lambda e, tc=tc: e.matmul(
            c.pa[0:8, 0:1], m1b_all[:, tc * 8:(tc + 1) * 8], ones_col128(c),
            start=(tc == 0), stop=(tc == NT - 1)),
             waits=([(s["dve"], m1_dve)] if tc == 0 else []))
    P.mark("tensor", s["pe"])
    pev = s["pe"].n
    P.on("vector", lambda e: e.tensor_copy(c.f32a[0:8, 520:521], c.pa[0:8, 0:1]),
         waits=[(s["pe"], pev)], inc=s["dve"])
    dvc = s["dve"].n
    P.on("sync", lambda e: e.dma_start(out=c.cnt_cc_in, in_=c.f32a[0:8, 520:521]),
         waits=[(s["dve"], dvc)], inc=s["st"], inc_n=16)
    stv = s["st"].n
    ccC = cc_next(c)
    P.on("gpsimd", lambda e: e.collective_compute(
        "AllGather", ALU.bypass, replica_groups=[list(range(NCORE))],
        ins=[c.cnt_cc_in.opt()], outs=[c.cnt_cc_out.opt()]),
         waits=[(s["st"], stv)], inc=ccC)
    cnt_r = c.cnt_cc_out.rearrange("(c e) x -> c (e x)", e=8)
    P.on("gpsimd", lambda e: e.dma_start(out=c.cnt8[:, :], in_=cnt_r),
         waits=[(ccC, 1)], inc=s["cast"], inc_n=16)
    cld = s["cast"].n
    P.on("tensor", lambda e: e.matmul(c.pb[0:1, 0:8], c.selb[:, :], c.cnt8[:, :],
                                      start=True, stop=True),
         waits=[(s["cast"], cld)], inc=s["pe"])
    pev2 = s["pe"].n
    P.on("vector", lambda e: e.tensor_copy(c.row_f[0:1, 56:64], c.pb[0:1, 0:8]),
         waits=[(s["pe"], pev2)], inc=s["dve"])
    dvo = s["dve"].n
    P.on("scalar", lambda e: e.activation(c.row_fb[0:1, 40:48], c.row_f[0:1, 56:64],
                                          AF.Copy), waits=[(s["dve"], dvo)], inc=s["act"])
    offs_cast = s["act"].n
    for tc in range(NT):
        bank = c.pw0 if tc % 2 == 0 else c.pw1
        for sc_ in range(tc):
            P.on("tensor", lambda e, sc_=sc_, bank=bank: e.matmul(
                bank[:, 0:8], c.pkb.sl(c.smb, "ones128t"),
                m1b_all[:, sc_ * 8:(sc_ + 1) * 8], start=(sc_ == 0), stop=False),
                 waits=([(s["dve"], c._sw_free.get(tc % 2, 0))] if sc_ == 0 else []))
        P.on("tensor", lambda e, tc=tc, bank=bank: e.matmul(
            bank[:, 0:8], c.pkb.sl(c.smb, "tri"), m1b_all[:, tc * 8:(tc + 1) * 8],
            start=(tc == 0), stop=False),
             waits=([(s["dve"], c._sw_free.get(tc % 2, 0))] if tc == 0 else []))
        P.on("tensor", lambda e, bank=bank: e.matmul(
            bank[:, 0:8], ones_row(c), c.row_fb[0:1, 40:48], start=False, stop=True),
             waits=[(s["act"], offs_cast)])
        P.mark("tensor", s["pe"])
        pev3 = s["pe"].n
        kp = c.f32a[:, 640:648]
        sm_t = c.f32a[:, tc * 8:(tc + 1) * 8]
        m1f = c.f32a[:, 1024 + tc * 8:1024 + (tc + 1) * 8]
        gate = c.f32a[:, 256 + tc:257 + tc]
        P.on("vector", lambda e, bank=bank: e.tensor_scalar(
            kp, bank[:, 0:8], CAPF, None, ALU.is_lt), waits=[(s["pe"], pev3)])
        P.on("vector", lambda e, m1f=m1f: e.tensor_tensor(kp, kp, m1f, ALU.mult))
        P.on("vector", lambda e, gate=gate, sm_t=sm_t: e.tensor_scalar(
            sm_t, kp, gate, None, ALU.mult), inc=s["dve"])
        if not hasattr(c, "_sw_free"):
            c._sw_free = {}
        c._sw_free[tc % 2] = s["dve"].n


def emit_gx(c):
    P, s = c.P, c.s
    ev_slot = [0, 0]
    lnscr_pe = [0]
    for tt in range(NT):
        xt = c.x_sb[:, tt * D:(tt + 1) * D]
        rstd = c.stat[:, tt:tt + 1]
        nmr = c.stat[:, 32 + tt:33 + tt]
        gate = c.f32a[:, 256 + tt:257 + tt]
        P.on("scalar", lambda e, xt=xt, r=rstd, b=nmr: e.activation(
            c.ln_scr[:], xt, AF.Identity, bias=b, scale=r),
             waits=[(s["pe"], lnscr_pe[0])], inc=s["act"])
        avA = s["act"].n
        P.on("vector", lambda e, gate=gate: e.tensor_scalar(
            c.ln_scr[:], c.ln_scr[:], gate, None, ALU.mult),
             waits=[(s["act"], avA)], inc=s["dve"])
        dvA = s["dve"].n
        tile_transpose_to_T(c, tt, dvA, ev_slot, lnscr_pe)


def emit_experts(c, kind, l=0):
    P, s = c.P, c.s
    if kind == "inner":
        halves, T, jc = 2, 1024, JC
        actf = AF.Gelu
        b1name = f"I{l}_b1"

        def b1sl(e_, j):
            return c.pk.sl(c.sm, b1name, c0=e_ * 32 + j, ncols=1)

        def scol(e_, tc):
            return c.f32a[:, tc * 8 + e_:tc * 8 + e_ + 1]
    else:
        halves, T, jc = 1, 2048, FC
        actf = AF.Relu

        def b1sl(e_, j):
            return c.pk.sl(c.sm, "L1_b1", c0=e_ * 8 + j, ncols=1)

        def scol(e_, tc):
            return c.f32a[:, tc * 8 + e_:tc * 8 + e_ + 1]

    nsl = T // 512
    g = c.arena[:, G_OFF:G_OFF + jc * T]
    w2slot = (lambda i: c.arena[:, W2_OFF + (i % 2) * 2048:W2_OFF + (i % 2) * 2048 + 2048]) \
        if kind == "inner" else None
    if kind == "l1":
        w2buf = c.arena[:, G_OFF + 16384:G_OFF + 24576]
        w1buf = c.arena[:, G_OFF + 24576:G_OFF + 32768]

    st8 = {"w1": [0, 0], "w2": [0, 0], "gelu": {}, "g_free": 0,
           "stg": [0, 0], "add": {}, "esi": 0, "w1d_l1": 0}

    def w1_load(e_, j):
        slot = c.wring[:, (j % 2) * D:(j % 2) * D + D]
        src = c.wd["I_ew1"][l, e_, j]
        P.on("gpsimd", lambda e, src=src, slot=slot: e.dma_start(out=slot, in_=src),
             waits=[(s["pe"], st8["w1"][j % 2])], inc=s["cast"], inc_n=16)
        return s["cast"].n, slot

    def l1_w1_load(e_, wait_pe):
        for f in range(FC):
            P.on("gpsimd", lambda e, f=f, e_=e_: e.dma_start(
                out=w1buf[:, f * D:(f + 1) * D], in_=c.wd["L1_W1"][e_, f]),
                 waits=([(s["pe"], wait_pe)] if f == 0 else []),
                 inc=s["cast"], inc_n=16)
        st8["w1d_l1"] = s["cast"].n

    if kind == "l1":
        l1_w1_load(0, 0)
    for half in range(halves):
        for e_ in ([0] if DBG_ONEEXP else range(E)):
            if kind == "l1":
                wb = w1buf
                w1d = st8["w1d_l1"]
            for j in range(jc):
                if kind == "inner":
                    w1d, slot = w1_load(e_, j)
                if T == 1024:
                    banks = [(c.pw0 if j % 2 == 0 else c.pw1, 0)]
                else:
                    banks = [(c.pw0, 0), (c.pw1, 1024)]
                bkey = (j % 2) if T == 1024 else 0
                for bi_, (bank, toff) in enumerate(banks):
                    for n2 in range(2):
                        tof = toff + n2 * 512
                        for f in range(FC):
                            if kind == "inner":
                                lh = slot[:, f * 128:(f + 1) * 128]
                            else:
                                lh = wb[:, f * D + j * 128:f * D + (j + 1) * 128]
                            P.on("tensor", lambda e, lh=lh, f=f, tof=tof, bank=bank,
                                 half=half: e.matmul(
                                bank[:, (tof % 1024):(tof % 1024) + 512],
                                lh,
                                c.xnT[:, f * R + half * T + tof: f * R + half * T + tof + 512],
                                start=(f == 0), stop=(f == FC - 1)),
                                 waits=([(s["cast"], w1d),
                                         (s["act"], st8["gelu"].get((bkey, bi_), 0))]
                                        if f == 0 else []))
                P.mark("tensor", s["pe"])
                st8["w1"][j % 2] = s["pe"].n
                mmv = s["pe"].n
                for bi_, (bank, toff) in enumerate(banks):
                    P.on("scalar", lambda e, bank=bank, j=j, toff=toff, e_=e_: e.activation(
                        g[:, j * T + toff: j * T + toff + 1024], bank[:, 0:1024],
                        actf, bias=b1sl(e_, j), scale=1.0),
                         waits=[(s["pe"], mmv), (s["pe"], st8["g_free"])],
                         inc=s["act"])
                    st8["gelu"][(bkey, bi_)] = s["act"].n
            if kind == "l1" and e_ + 1 < E:
                l1_w1_load(e_ + 1, s["pe"].n)
            gelu_all = s["act"].n

            if kind == "l1":
                for f in range(FC):
                    P.on("gpsimd", lambda e, f=f, e_=e_: e.dma_start(
                        out=w2buf[:, f * D:(f + 1) * D], in_=c.wd["L1_W2"][e_, f]),
                         waits=([(s["pe"], st8["w2"][0])] if f == 0 else []),
                         inc=s["cast"], inc_n=16)
                w2d = s["cast"].n

            def w2_half_load(oc, hh):
                si = (oc * 2 + hh) % 2
                P.on("gpsimd", lambda e, oc=oc, hh=hh, si=si, e_=e_: e.dma_start(
                    out=w2slot(si), in_=c.wd["I_ew2"][l, e_, oc][:, hh * 2048:(hh + 1) * 2048]),
                     waits=[(s["pe"], st8["w2"][si])], inc=s["cast"], inc_n=16)
                return s["cast"].n, si

            def evac_out(oc, n, bank, bkkey, omv, e_=e_, half=half):
                si2 = st8["esi"]
                st8["esi"] = (si2 + 1) % 2
                ssl = c.stage[:, si2 * 1024: si2 * 1024 + 512]
                P.on("vector", lambda e: e.tensor_copy(ssl, bank[:]),
                     waits=[(s["pe"], omv), (s["pe"], st8["stg"][si2])],
                     inc=s["dve"])
                evv = s["dve"].n
                ptb = c.pt0 if si2 == 0 else c.pt1
                for k in range(4):
                    P.on("tensor", lambda e, k=k: e.transpose(
                        ptb[:, k * 128:(k + 1) * 128], ssl[:, k * 128:(k + 1) * 128],
                        c.pkb.sl(c.smb, "ident")),
                         waits=([(s["dve"], evv),
                                 (s["dve"], st8["add"].get(("t", si2), 0))]
                                if k == 0 else []))
                P.mark("tensor", s["pe"])
                st8["stg"][si2] = s["pe"].n
                trv = s["pe"].n
                for k in range(4):
                    tc = half * (T // 128) + n * 4 + k
                    tmp = c.ln_scr[:, si2 * 512 + k * 128: si2 * 512 + (k + 1) * 128]
                    xsl = c.x_sb[:, tc * D + oc * 128: tc * D + (oc + 1) * 128]
                    P.on("vector", lambda e, tmp=tmp, k=k, tc=tc: e.tensor_scalar(
                        tmp, ptb[:, k * 128:(k + 1) * 128], scol(e_, tc),
                        None, ALU.mult),
                         waits=([(s["pe"], trv)] if k == 0 else []))
                    P.on("vector", lambda e, xsl=xsl, tmp=tmp: e.tensor_tensor(
                        xsl, xsl, tmp, ALU.add))
                P.on("vector", lambda e: e.engine_nop(), inc=s["dve"])
                st8["add"][bkkey] = s["dve"].n
                st8["add"][("t", si2)] = s["dve"].n

            if kind == "inner":
                if DBG_W2NAIVE:
                    for oc in range(FC):
                        full = c.arena[:, W2_OFF:W2_OFF + 4096]
                        P.on("gpsimd", lambda e, oc=oc, e_=e_: e.dma_start(
                            out=full, in_=c.wd["I_ew2"][l, e_, oc]),
                             waits=[(s["pe"], max(st8["w2"][0], st8["w2"][1]))],
                             inc=s["cast"], inc_n=16)
                        w2dn = s["cast"].n
                        for jj in range(jc):
                            for n in range(nsl):
                                bank = c.pa if n == 0 else c.pb
                                lh = full[:, jj * 128:(jj + 1) * 128]
                                P.on("tensor", lambda e, lh=lh, jj=jj, n=n, bank=bank: e.matmul(
                                    bank[:], lh, g[:, jj * T + n * 512:jj * T + (n + 1) * 512],
                                    start=(jj == 0), stop=(jj == jc - 1)),
                                     waits=([(s["cast"], w2dn), (s["act"], gelu_all),
                                             (s["dve"], st8["add"].get(n, 0))]
                                            if jj == 0 else []))
                        P.mark("tensor", s["pe"])
                        st8["w2"][0] = st8["w2"][1] = s["pe"].n
                        omv = s["pe"].n
                        evac_out(oc, 0, c.pa, 0, omv)
                        evac_out(oc, 1, c.pb, 1, omv)
                    st8["g_free"] = s["pe"].n
                    continue
                w2h = {(0, 0): w2_half_load(0, 0), (0, 1): w2_half_load(0, 1)}
                for oc in range(FC):
                    jjrange = [0] if DBG_W2ONE else list(range(jc))
                    for jj in jjrange:
                        hh = jj // 16
                        w2d_h, si = w2h[(oc, hh)]
                        for n in range(nsl):
                            bank = c.pa if n == 0 else c.pb
                            lh = w2slot(si)[:, (jj % 16) * 128:(jj % 16 + 1) * 128]
                            P.on("tensor", lambda e, lh=lh, jj=jj, n=n, bank=bank: e.matmul(
                                bank[:], lh, g[:, jj * T + n * 512:jj * T + (n + 1) * 512],
                                start=(jj == 0), stop=(jj == jc - 1 or DBG_W2ONE)),
                                 waits=([(s["cast"], w2d_h), (s["act"], gelu_all),
                                         (s["dve"], st8["add"].get(n, 0))]
                                        if jj % 16 == 0 else []))
                        if jj == 15 or jj == 31:
                            P.mark("tensor", s["pe"])
                            st8["w2"][si] = s["pe"].n
                            if jj == 31 and oc + 1 < FC:
                                w2h[(oc + 1, 0)] = w2_half_load(oc + 1, 0)
                                w2h[(oc + 1, 1)] = w2_half_load(oc + 1, 1)
                    omv = s["pe"].n
                    evac_out(oc, 0, c.pa, 0, omv)
                    evac_out(oc, 1, c.pb, 1, omv)
            else:
                for oc in range(FC):
                    for n in range(nsl):
                        bank = c.pa if (oc * nsl + n) % 2 == 0 else c.pb
                        for jj in range(jc):
                            lh = w2buf[:, jj * D + oc * 128:jj * D + (oc + 1) * 128]
                            P.on("tensor", lambda e, lh=lh, jj=jj, n=n, bank=bank: e.matmul(
                                bank[:], lh, g[:, jj * T + n * 512:jj * T + (n + 1) * 512],
                                start=(jj == 0), stop=(jj == jc - 1)),
                                 waits=([(s["cast"], w2d), (s["act"], gelu_all),
                                         (s["dve"], st8["add"].get((oc * nsl + n) % 2, 0))]
                                        if jj == 0 else []))
                        P.mark("tensor", s["pe"])
                        st8["w2"][0] = s["pe"].n
                        evac_out(oc, n, bank, (oc * nsl + n) % 2, s["pe"].n)
            st8["g_free"] = s["pe"].n


def emit_b2_term(c, b2col):
    P, s = c.P, c.s
    P.on("gpsimd", lambda e: e.dma_start(out=c.b2sb[:, :],
                                         in_=c.b2s_d[:, b2col:b2col + D]),
         inc=s["cast"], inc_n=16)
    b2cast = s["cast"].n
    cwb = c.ln_scr[:, 0:128]
    st = {"cwb_pe": 0, "pt_dve": 0, "add": {}}
    for tc in range(NT):
        P.on("scalar", lambda e, tc=tc: e.activation(
            cwb[:, 0:8], c.f32a[:, tc * 8:(tc + 1) * 8], AF.Copy),
             waits=[(s["pe"], st["cwb_pe"])], inc=s["act"])
        av = s["act"].n
        P.on("tensor", lambda e, tc=tc: e.transpose(
            c.pt0[0:8, 0:128], cwb[:, 0:8], c.pkb.sl(c.smb, "ident")),
             waits=[(s["act"], av), (s["dve"], st["pt_dve"])], inc=s["pe"])
        pev = s["pe"].n
        st["cwb_pe"] = pev
        P.on("vector", lambda e, tc=tc: e.tensor_copy(
            c.stage[0:8, tc * 128:(tc + 1) * 128], c.pt0[0:8, 0:128]),
             waits=[(s["pe"], pev)], inc=s["dve"])
        st["pt_dve"] = s["dve"].n
    cw_dve = s["dve"].n
    for tc in range(NT):
        for no in range(2):
            bank = c.pa if (tc * 2 + no) % 2 == 0 else c.pb
            P.on("tensor", lambda e, tc=tc, no=no, bank=bank: e.matmul(
                bank[:], c.stage[0:8, tc * 128:(tc + 1) * 128],
                c.b2sb[0:8, no * 512:(no + 1) * 512],
                start=True, stop=True),
                 waits=[(s["dve"], cw_dve), (s["cast"], b2cast),
                        (s["dve"], st["add"].get((tc * 2 + no) % 2, 0))],
                 inc=s["pe"])
            pev = s["pe"].n
            xsl = c.x_sb[:, tc * D + no * 512: tc * D + (no + 1) * 512]
            P.on("vector", lambda e, xsl=xsl, bank=bank: e.tensor_tensor(
                xsl, xsl, bank[:], ALU.add), waits=[(s["pe"], pev)], inc=s["dve"])
            st["add"][(tc * 2 + no) % 2] = s["dve"].n


def kernel(**inputs):
    maps, pk, pkb, pr = prep_host(inputs)
    c = build_nc(pk, pkb, pr, blocks=ALL_BLOCKS)
    res = run_bass_kernel_spmd(c.nc, maps, core_ids=list(range(NCORE)))
    out = np.concatenate([res.results[i]["x_out"] for i in range(NCORE)], axis=0)
    return out.reshape(BATCH, SEQ, D).astype(np.float32)
